# version 04
# baseline (speedup 1.0000x reference)
import contextlib
import numpy as np
import ml_dtypes
import concourse.bass as bass
import concourse.mybir as mybir
from concourse.bass_utils import run_bass_kernel_spmd

F32 = mybir.dt.float32
BF16 = mybir.dt.bfloat16
AF = mybir.ActivationFunctionType
ALU = mybir.AluOpType
AX = mybir.AxisListType
BFnp = ml_dtypes.bfloat16

T = 8192
D = 1024
NT = T // 512
ENGS = ("tensor", "vector", "scalar", "gpsimd", "sync")
SQ2I = 0.7071067811865476



class K:
    def __init__(self):
        self.nc = bass.Bass("TRN2", target_bir_lowering=False)
        self.stack = contextlib.ExitStack()
        self.cur = self.stack
        self.streams = {e: [] for e in ENGS}
        self.sems = {}
        self.cnt = {}
        self._uid = 0

    def uid(self):
        self._uid += 1
        return self._uid

    @contextlib.contextmanager
    def scope(self):
        prev = self.cur
        with contextlib.ExitStack() as st:
            self.cur = st
            try:
                yield st
            finally:
                self.cur = prev

    def sem(self, name):
        if name not in self.sems:
            self.sems[name] = self.stack.enter_context(self.nc.semaphore(name))
            self.cnt[name] = 0
        return self.sems[name]

    def sbuf(self, name, shape, dt):
        return self.cur.enter_context(
            self.nc.sbuf_tensor(f"{name}_{self.uid()}", shape, dt))

    def psum(self, name, shape, dt):
        return self.cur.enter_context(
            self.nc.psum_tensor(f"{name}_{self.uid()}", shape, dt))

    def dram(self, name, shape, dt, kind="Internal"):
        return self.nc.dram_tensor(name, shape, dt, kind=kind).ap()

    def op(self, eng, fn, waits=(), inc=None, incn=1):
        tgt = None
        if inc is not None:
            self.sem(inc)
            self.cnt[inc] += incn
            tgt = self.cnt[inc]
        for s, _ in waits:
            self.sem(s)
        sems = self.sems

        def run(e, fn=fn, waits=tuple(waits), inc=inc, incn=incn):
            for s, v in waits:
                e.wait_ge(sems[s], v)
            inst = fn(e)
            if inc is not None:
                inst.then_inc(sems[inc], incn)

        self.streams[eng].append(run)
        return tgt

    def dma(self, eng, out, in_, waits=(), sem=None, **kw):
        if sem is None:
            sem = "d_" + eng
        return self.op(eng, lambda e, o=out, i=in_, kw=kw: e.dma_start(o, i, **kw),
                       waits=waits, inc=sem, incn=16)

    def barrier(self):
        self.sem("B")
        sems = self.sems
        for eng in ("sync", "gpsimd", "scalar"):
            dn = "d_" + eng
            if dn in self.cnt:
                v = self.cnt[dn]
                self.streams[eng].append(
                    lambda e, s=dn, v=v: e.wait_ge(sems[s], v))
        self.cnt["B"] += 5
        tgt = self.cnt["B"]
        for eng in ENGS:
            def f(e, tgt=tgt):
                e.sem_inc(sems["B"], 1)
                e.wait_ge(sems["B"], tgt)
            self.streams[eng].append(f)

    def finalize(self):
        with self.nc.Block() as block:
            for engname in ENGS:
                dec = getattr(block, engname)
                lst = self.streams[engname]

                def mk(lst):
                    def _(eng):
                        for f in lst:
                            f(eng)
                    return _
                dec(mk(lst))
        self.stack.close()
        return self.nc


def setup_consts(k):
    nc = k.nc
    cst = k.sbuf("cst", [128, 8], F32)
    idf = k.sbuf("idf", [128, 128], F32)
    idb = k.sbuf("idb", [128, 128], BF16)
    consts_d = k.dram("consts", [128, 8], F32, kind="ExternalInput")
    ident_d = k.dram("ident", [128, 128], F32, kind="ExternalInput")
    k.dma("sync", cst[:], consts_d)
    k.dma("sync", idf[:], ident_d)
    k.dma("gpsimd", idb[:], ident_d)
    for val, col in ((0.0, 0), (1.0, 1), (0.5, 2), (-1.0, 3), (1e-5, 4),
                     (0.25, 5), (1.0 / 1024.0, 6), (-0.5, 7)):
        nc.const_aps.register(F32, val, cst[:, col:col + 1])
    onesb = k.sbuf("onesb", [128, 128], BF16)
    onesm_d = k.dram("onesm", [128, 128], F32, kind="ExternalInput")
    k.dma("gpsimd", onesb[:], onesm_d)
    i7 = k.sbuf("i7", [64, 64], BF16)
    i15 = k.sbuf("i15", [64, 64], BF16)
    i13 = k.sbuf("i13", [64, 64], BF16)
    dv = k.cnt["d_gpsimd"]
    k.op("vector", lambda e: e.tensor_scalar(
        i7[:], idb[0:64, 0:64], 7.0, None, ALU.mult), waits=[("d_gpsimd", dv)])
    k.op("vector", lambda e: e.tensor_scalar(
        i15[:], idb[0:64, 0:64], 15.0, None, ALU.mult))
    k.op("vector", lambda e: e.tensor_scalar(
        i13[:], idb[0:64, 0:64], 13.0, None, ALU.mult), inc="V")
    k.barrier()
    return dict(cst=cst, idf=idf, idb=idb, onesb=onesb, i7=i7, i15=i15,
                i13=i13)


def host_consts():
    c = np.zeros((128, 8), np.float32)
    for v, i in ((0.0, 0), (1.0, 1), (0.5, 2), (-1.0, 3), (1e-5, 4),
                 (0.25, 5), (1.0 / 1024.0, 6), (-0.5, 7)):
        c[:, i] = v
    return c




def emit_ln(k, C, x_d, xnT_d, nt=NT):
    idb = C["idb"]
    with k.scope():
        xt = [k.sbuf("ln_xt", [128, 4, 1024], F32) for _ in range(2)]
        xn = k.sbuf("ln_xn", [128, 4, 1024], BF16)
        st = k.sbuf("ln_st", [128, 16], F32)
        sq = k.sbuf("ln_sq", [128, 4, 1024], F32)
        xnT = [k.sbuf("ln_xnT", [128, 8, 512], BF16) for _ in range(2)]
        pT = [k.psum("ln_pT", [128, 128], BF16) for _ in range(2)]

        xr = x_d.rearrange("(ti s p) d -> ti p s d", s=4, p=128)
        dstr = xnT_d.rearrange("(kb p) t -> p kb t", p=128)

        vfree = {0: 0, 1: 0}
        sqfree = 0
        outdma = {0: 0, 1: 0}
        for ti in range(nt):
            b = ti % 2
            ld = k.dma("sync", xt[b][:], xr[ti], waits=[("V", vfree[b])])
            v0 = k.op("vector", lambda e, b=b: e.tensor_reduce(
                st[:, 0:4], xt[b][:], AX.X, ALU.add), waits=[("d_sync", ld)],
                inc="V")
            ssq = k.op("scalar", lambda e, b=b: e.activation(
                sq[:], xt[b][:], AF.Square),
                waits=[("d_sync", ld), ("V", sqfree)], inc="S")
            sqfree = k.op("vector", lambda e: e.tensor_reduce(
                st[:, 4:8], sq[:], AX.X, ALU.add), waits=[("S", ssq)], inc="V")
            v1 = k.op("vector", lambda e: e.tensor_scalar(
                st[:, 8:12], st[:, 0:4], 1.0 / 1024.0, None, ALU.mult),
                waits=[("V", v0)], inc="V")
            v2 = k.op("vector", lambda e: e.tensor_tensor(
                st[:, 12:16], st[:, 8:12], st[:, 8:12], ALU.mult),
                waits=[("V", v1)], inc="V")
            v3 = k.op("vector", lambda e: e.tensor_scalar(
                st[:, 0:4], st[:, 4:8], 1.0 / 1024.0, None, ALU.mult),
                waits=[("V", sqfree)], inc="V")
            vv = k.op("vector", lambda e: e.tensor_tensor(
                st[:, 4:8], st[:, 0:4], st[:, 12:16], ALU.subtract),
                waits=[("V", v3), ("V", v2)], inc="V")
            s1v = k.op("scalar", lambda e: e.activation(
                st[:, 4:8], st[:, 4:8], AF.Ln, bias=1e-5), waits=[("V", vv)],
                inc="S")
            sv = k.op("scalar", lambda e: e.activation(
                st[:, 4:8], st[:, 4:8], AF.Exp, scale=-0.5),
                waits=[("S", s1v)], inc="S")
            for s in range(4):
                vv = k.op("vector", lambda e, b=b, s=s: e.tensor_scalar(
                    xn[:, s, :], xt[b][:, s, :], st[:, 8 + s:9 + s],
                    st[:, 4 + s:5 + s], ALU.subtract, ALU.mult),
                    waits=[("S", sv)], inc="V")
            vfree[b] = vv
            for s in range(4):
                for kb in range(8):
                    pb = (s * 8 + kb) % 2
                    tv = k.op("tensor", lambda e, s=s, kb=kb, pb=pb: e.transpose(
                        pT[pb][:], xn[:, s, kb * 128:(kb + 1) * 128], idb[:]),
                        waits=[("V", k.cnt["V"])], inc="T")
                    k.op("vector", lambda e, b=b, s=s, kb=kb, pb=pb: e.tensor_copy(
                        xnT[b][:, kb, s * 128:(s + 1) * 128], pT[pb][:]),
                        waits=[("T", tv), ("d_sync", outdma[b])], inc="V")
            outdma[b] = k.dma(
                "sync", dstr[:, :, ti * 512:(ti + 1) * 512], xnT[b][:],
                waits=[("V", k.cnt["V"])])
        k.barrier()




def emit_mm_stage(k, C, src_d, wT_d, kdim, oslice, evac, nt=NT):
    nkb = kdim // 128
    nob = (oslice[1] - oslice[0]) // 128
    with k.scope():
        w_sb = k.sbuf("st_w", [128, nkb, nob * 128], BF16)
        rhs = [k.sbuf("st_rhs", [128, nkb, 512], BF16) for _ in range(2)]
        ps = [k.psum("st_ps", [128, 512], F32) for _ in range(4)]

        wr = wT_d.rearrange("(kb p) o -> p kb o", p=128)[:, :, oslice[0]:oslice[1]]
        wv = k.dma("gpsimd", w_sb[:], wr)
        srcr = src_d.rearrange("(kb p) t -> p kb t", p=128)

        bankfree = {i: ("V", 0) for i in range(4)}
        rhsfree = {0: 0, 1: 0}
        for ti in range(nt):
            b = ti % 2
            ld = k.dma("sync", rhs[b][:], srcr[:, :, ti * 512:(ti + 1) * 512],
                       waits=[("T", rhsfree[b])])
            for ob in range(nob):
                pb = ob % 4
                for kb in range(nkb):
                    wl = []
                    if kb == 0:
                        wl = [("d_sync", ld), ("d_gpsimd", wv), bankfree[pb]]
                    mmv = k.op("tensor", lambda e, kb=kb, ob=ob, b=b, pb=pb:
                               e.matmul(ps[pb][:],
                                        w_sb[:, kb, ob * 128:(ob + 1) * 128],
                                        rhs[b][:, kb, :], start=(kb == 0),
                                        stop=(kb == nkb - 1)),
                               waits=wl, inc="T" if kb == nkb - 1 else None)
                bankfree[pb] = evac(ti, ob, ps[pb], mmv)
            rhsfree[b] = k.cnt["T"]
    k.barrier()




def emit_l0ffn(k, C, x_in, x_out, W, nt=NT):
    u = k.uid()
    xnT = k.dram(f"ffn_xnT{u}", [1024, T], BF16)
    hgT = k.dram(f"ffn_hgT{u}", [4096, T], BF16)
    g1T = k.dram(f"ffn_g1T{u}", [4096, T], BF16)
    g2T = k.dram(f"ffn_g2T{u}", [4096, T], BF16)
    s1 = W["s1"]; s2 = W["s2"]

    emit_ln(k, C, x_in, xnT, nt=nt)

    with k.scope():
        nkb = 8
        w1_sb = k.sbuf("f1w1", [128, nkb, 4096], BF16)
        w2_sb = k.sbuf("f1w2", [128, nkb, 4096], BF16)
        rhs = [k.sbuf("f1rhs", [128, nkb, 512], BF16) for _ in range(2)]
        z1 = k.sbuf("f1z1", [128, 512], BF16)
        z2 = k.sbuf("f1z2", [128, 512], BF16)
        sg = k.sbuf("f1sg", [128, 512], BF16)
        t1 = k.sbuf("f1t1", [128, 512], BF16)
        hg = [k.sbuf("f1hg", [128, 16, 512], BF16) for _ in range(2)]
        psA = [k.psum("f1A", [128, 512], F32) for _ in range(2)]
        psC = [k.psum("f1C", [128, 512], F32) for _ in range(2)]

        k.dma("gpsimd", w1_sb[:], W["w1T"].rearrange("(kb p) o -> p kb o", p=128))
        wv = k.dma("gpsimd", w2_sb[:], W["w2T"].rearrange("(kb p) o -> p kb o", p=128))
        srcr = xnT.rearrange("(kb p) t -> p kb t", p=128)
        dstr = hgT.rearrange("(g ow p) t -> g p ow t", ow=16, p=128)

        freeA = {0: ("S", 0), 1: ("S", 0)}
        freeC = {0: ("S", 0), 1: ("S", 0)}
        rhsfree = {0: 0, 1: 0}
        hgdma = {0: 0, 1: 0}
        lastvh = [0]
        for ti in range(nt):
            b = ti % 2
            ld = k.dma("sync", rhs[b][:], srcr[:, :, ti * 512:(ti + 1) * 512],
                       waits=[("T", rhsfree[b])])
            for g in range(2):
                hb = (ti * 2 + g) % 2
                for oo in range(16):
                    ob = g * 16 + oo
                    ab = ob % 2
                    for kb in range(nkb):
                        wl = [("d_sync", ld), ("d_gpsimd", wv), freeA[ab]] \
                            if kb == 0 else []
                        mv1 = k.op("tensor", lambda e, kb=kb, ob=ob, b=b,
                                   ab=ab: e.matmul(
                                       psA[ab][:],
                                       w1_sb[:, kb, ob * 128:(ob + 1) * 128],
                                       rhs[b][:, kb, :], start=(kb == 0),
                                       stop=(kb == nkb - 1)),
                                   waits=wl, inc="T" if kb == nkb - 1 else None)
                    for kb in range(nkb):
                        wl = [freeC[ab]] if kb == 0 else []
                        mv2 = k.op("tensor", lambda e, kb=kb, ob=ob, b=b,
                                   ab=ab: e.matmul(
                                       psC[ab][:],
                                       w2_sb[:, kb, ob * 128:(ob + 1) * 128],
                                       rhs[b][:, kb, :], start=(kb == 0),
                                       stop=(kb == nkb - 1)),
                                   waits=wl, inc="T" if kb == nkb - 1 else None)
                    sz1 = k.op("scalar", lambda e, ob=ob, ab=ab: e.activation(
                        z1[:], psA[ab][:], AF.Identity,
                        scale=s1[:, ob:ob + 1]),
                        waits=[("T", mv1), ("V", lastvh[0])], inc="S")
                    sv = k.op("scalar", lambda e: e.activation(
                        sg[:], z1[:], AF.Sigmoid), waits=[("S", sz1)], inc="S")
                    freeA[ab] = ("S", sv)
                    sz2 = k.op("scalar", lambda e, ob=ob, ab=ab: e.activation(
                        z2[:], psC[ab][:], AF.Identity,
                        scale=s2[:, ob:ob + 1]), waits=[("T", mv2)], inc="S")
                    freeC[ab] = ("S", sz2)
                    k.op("vector", lambda e: e.tensor_tensor(
                        t1[:], z1[:], sg[:], ALU.mult), waits=[("S", sz2)])
                    vh = k.op("vector", lambda e, hb=hb, oo=oo: e.tensor_tensor(
                        hg[hb][:, oo, :], t1[:], z2[:], ALU.mult),
                        waits=[("d_sync", hgdma[hb])] if oo == 0 else (),
                        inc="V")
                    lastvh[0] = vh
                hgdma[hb] = k.dma(
                    "sync", dstr[g][:, :, ti * 512:(ti + 1) * 512], hg[hb][:],
                    waits=[("V", vh)])
            rhsfree[b] = k.cnt["T"]
    k.barrier()

    def gelu_stage(src, dst, wT, shh, she):
        dstr = dst.rearrange("(q ow p) t -> q p ow t", ow=8, p=128)
        for q in range(4):
            with k.scope():
                zq = k.sbuf("g_z", [128, 512], BF16)
                eq = k.sbuf("g_e", [128, 512], BF16)
                gbuf = [k.sbuf("g_b", [128, 8, 512], BF16) for _ in range(2)]
                state = dict(dma={0: 0, 1: 0}, vh=0)

                def evac(ti, ob, psum, mmv, q=q, zq=zq, eq=eq, gbuf=gbuf,
                         state=state, shh=shh, she=she):
                    b = ti % 2
                    col = q * 8 + ob
                    s1v = k.op("scalar", lambda e: e.activation(
                        zq[:], psum[:], AF.Identity, scale=shh[:, col:col + 1]),
                        waits=[("T", mmv), ("V", state["vh"])], inc="S")
                    sv = k.op("scalar", lambda e: e.activation(
                        eq[:], psum[:], AF.Erf, scale=she[:, col:col + 1]),
                        waits=[("S", s1v)], inc="S")
                    vh = k.op("vector", lambda e: e.scalar_tensor_tensor(
                        gbuf[b][:, ob, :], eq[:], 1.0, zq[:], ALU.add,
                        ALU.mult),
                        waits=[("S", sv)] + ([("d_sync", state["dma"][b])]
                                             if ob == 0 else []), inc="V")
                    state["vh"] = vh
                    if ob == 7:
                        state["dma"][b] = k.dma(
                            "sync", dstr[q][:, :, ti * 512:(ti + 1) * 512],
                            gbuf[b][:], waits=[("V", vh)])
                    return ("S", sv)

                emit_mm_stage(k, C, src, wT, 4096,
                              (q * 1024, (q + 1) * 1024), evac, nt=nt)

    gelu_stage(hgT, g1T, W["h1T"], W["sh1hh"], W["sh1e"])
    gelu_stage(g1T, g2T, W["h2T"], W["sh2hh"], W["sh2e"])

    with k.scope():
        idf = C["idf"]
        s3h = W["s3h"]
        xt = [k.sbuf("f4xt", [128, 4, 1024], F32) for _ in range(2)]
        tmp = k.sbuf("f4tmp", [128, 512], F32)
        TF = [k.psum("f4TF", [128, 128], F32) for _ in range(2)]
        xr = x_in.rearrange("(ti s p) d -> ti p s d", s=4, p=128)
        xo = x_out.rearrange("(ti s p) d -> ti p s d", s=4, p=128)
        st4 = dict(xdma={0: 0, 1: 0}, tffree={0: ("V", 0), 1: ("V", 0)},
                   tmpfree=("T", 0), ld=0)

        def evac4(ti, ob, psum, mmv):
            b = ti % 2
            if ob == 0:
                st4["ld"] = k.dma("sync", xt[b][:], xr[ti],
                                  waits=[("d_sync", st4["xdma"][b])])
            v1 = k.op("vector", lambda e, ob=ob: e.tensor_scalar(
                tmp[:], psum[:], s3h[:, ob:ob + 1], 0.5, ALU.mult, ALU.mult),
                waits=[("T", mmv), st4["tmpfree"]], inc="V")
            av = 0
            for s in range(4):
                pb = s % 2
                tv = k.op("tensor", lambda e, s=s, pb=pb: e.transpose(
                    TF[pb][:], tmp[:, s * 128:(s + 1) * 128], idf[:]),
                    waits=[("V", v1), st4["tffree"][pb]], inc="T")
                av = k.op("vector", lambda e, s=s, ob=ob, b=b, pb=pb:
                          e.tensor_tensor(
                              xt[b][:, s, ob * 128:(ob + 1) * 128],
                              xt[b][:, s, ob * 128:(ob + 1) * 128],
                              TF[pb][:], ALU.add),
                          waits=[("T", tv)] + ([("d_sync", st4["ld"])]
                                               if (s == 0 and ob == 0) else []),
                          inc="V")
                st4["tffree"][pb] = ("V", av)
            st4["tmpfree"] = ("T", k.cnt["T"])
            if ob == 7:
                st4["xdma"][b] = k.dma("sync", xo[ti], xt[b][:],
                                       waits=[("V", av)])
            return ("V", v1)

        emit_mm_stage(k, C, g2T, W["w3T"], 4096, (0, 1024), evac4, nt=nt)




def prep_weights_l0ffn(ins):
    f = np.float32
    ct = np.ascontiguousarray
    out = {}
    out["l0_w1T"] = ct(ins["L0_ffn_w1_w"].T.astype(f))
    out["l0_w2T"] = ct(ins["L0_ffn_w2_w"].T.astype(f))
    out["l0_h1T"] = ct(ins["L0_ffn_h1_w"].T.astype(f))
    out["l0_h2T"] = ct(ins["L0_ffn_h2_w"].T.astype(f))
    out["l0_w3T"] = ct(ins["L0_ffn_w3_w"].T.astype(f))
    out["l0_s1"] = ct(ins["L0_ffn_w1_s"].astype(f).reshape(32, 128).T)
    out["l0_s2"] = ct(ins["L0_ffn_w2_s"].astype(f).reshape(32, 128).T)
    out["l0_sh1"] = ct(ins["L0_ffn_h1_s"].astype(f).reshape(32, 128).T)
    out["l0_sh2"] = ct(ins["L0_ffn_h2_s"].astype(f).reshape(32, 128).T)
    out["l0_s3"] = ct(ins["L0_ffn_w3_s"].astype(f).reshape(8, 128).T)
    return out


def declare_l0ffn_weights(k, C):
    W = {}
    W["w1T"] = k.dram("l0_w1T", [1024, 4096], F32, kind="ExternalInput")
    W["w2T"] = k.dram("l0_w2T", [1024, 4096], F32, kind="ExternalInput")
    W["h1T"] = k.dram("l0_h1T", [4096, 4096], F32, kind="ExternalInput")
    W["h2T"] = k.dram("l0_h2T", [4096, 4096], F32, kind="ExternalInput")
    W["w3T"] = k.dram("l0_w3T", [4096, 1024], F32, kind="ExternalInput")
    decl = [("l0_s1", "s1", 32), ("l0_s2", "s2", 32), ("l0_sh1", "sh1h", 32),
            ("l0_sh2", "sh2h", 32), ("l0_s3", "s3h", 8)]
    for dn, key, nb in decl:
        d = k.dram(dn, [128, nb], F32, kind="ExternalInput")
        t = k.sbuf(dn + "_sb", [128, nb], F32)
        k.dma("sync", t[:], d)
        W[key] = t
    W["sh1e"] = k.sbuf("l0_sh1e", [128, 32], F32)
    W["sh2e"] = k.sbuf("l0_sh2e", [128, 32], F32)
    W["sh1hh"] = k.sbuf("l0_sh1hh", [128, 32], F32)
    W["sh2hh"] = k.sbuf("l0_sh2hh", [128, 32], F32)
    dv = k.cnt["d_sync"]
    k.op("vector", lambda e: e.tensor_scalar(
        W["sh1e"][:], W["sh1h"][:], SQ2I, None, ALU.mult),
        waits=[("d_sync", dv)])
    k.op("vector", lambda e: e.tensor_scalar(
        W["sh2e"][:], W["sh2h"][:], SQ2I, None, ALU.mult), inc="V")
    k.op("vector", lambda e: e.tensor_scalar(
        W["sh1hh"][:], W["sh1h"][:], 0.5, None, ALU.mult))
    k.op("vector", lambda e: e.tensor_scalar(
        W["sh2hh"][:], W["sh2h"][:], 0.5, None, ALU.mult), inc="V")
    k.barrier()
    return W


def build_launch_ffn0():
    k = K()
    x_in = k.dram("x_in", [T, D], F32, kind="ExternalInput")
    x_out = k.dram("x_out", [T, D], F32, kind="ExternalOutput")
    C = setup_consts(k)
    W = declare_l0ffn_weights(k, C)
    emit_l0ffn(k, C, x_in, x_out, W)
    return k.finalize()


def run_launch_ffn0(x_full, ins, n_cores=8):
    wp = prep_weights_l0ffn(ins)
    nc = build_launch_ffn0()
    shared = base_inputs()
    shared.update(wp)
    maps = []
    for c in range(n_cores):
        m = dict(shared)
        m["x_in"] = np.ascontiguousarray(x_full[c])
        maps.append(m)
    res = run_bass_kernel_spmd(nc, maps, core_ids=list(range(n_cores)))
    return np.stack([np.asarray(r["x_out"]) for r in res.results])


def kernel(**inputs):
    return kernel_fused(**inputs)




def prep_weights_moe(ins):
    f = np.float32
    ct = np.ascontiguousarray
    out = {}
    out["m_rW"] = ct(ins["L1_ffn_router_W"].astype(f))
    out["m_rb"] = ct(ins["L1_ffn_router_b"].astype(f).reshape(1, 15))
    out["m_sW1"] = ct(ins["L1_ffn_sW1"].astype(f))
    out["m_sW2"] = ct(ins["L1_ffn_sW2"].astype(f))
    out["m_sb1"] = ct(ins["L1_ffn_sb1"].astype(f).reshape(8, 128).T)
    out["m_sb2"] = ct(ins["L1_ffn_sb2"].astype(f).reshape(8, 128).T)
    out["m_eW1"] = ct(ins["L1_ffn_eW1"].astype(f))
    out["m_eW2"] = ct(ins["L1_ffn_eW2"].astype(f))
    eb1 = ins["L1_ffn_eb1"].astype(f)
    out["m_eb1"] = ct(np.concatenate(
        [eb1[e].reshape(8, 128).T for e in range(15)], axis=1))
    out["m_eb2"] = ct(ins["L1_ffn_eb2"].astype(f))
    out["m_ones1"] = np.ones((1, 128), np.float32)
    return out


def declare_moe_weights(k, C):
    W = {}
    W["rW"] = k.dram("m_rW", [1024, 15], F32, kind="ExternalInput")
    W["sW1"] = k.dram("m_sW1", [1024, 1024], F32, kind="ExternalInput")
    W["sW2"] = k.dram("m_sW2", [1024, 1024], F32, kind="ExternalInput")
    W["eW1"] = k.dram("m_eW1", [15, 1024, 1024], F32, kind="ExternalInput")
    W["eW2"] = k.dram("m_eW2", [15, 1024, 1024], F32, kind="ExternalInput")
    rb_d = k.dram("m_rb", [1, 15], F32, kind="ExternalInput")
    eb2_d = k.dram("m_eb2", [15, 1024], F32, kind="ExternalInput")
    ones_d = k.dram("m_ones1", [1, 128], F32, kind="ExternalInput")
    sb1_d = k.dram("m_sb1", [128, 8], F32, kind="ExternalInput")
    sb2_d = k.dram("m_sb2", [128, 8], F32, kind="ExternalInput")
    eb1_d = k.dram("m_eb1", [128, 120], F32, kind="ExternalInput")
    W["rb"] = k.sbuf("m_rb_sb", [1, 15], BF16)
    W["eb2"] = k.sbuf("m_eb2_sb", [15, 1024], BF16)
    W["ones1"] = k.sbuf("m_ones_sb", [1, 128], BF16)
    k.dma("gpsimd", W["rb"][:], rb_d)
    k.dma("gpsimd", W["eb2"][:], eb2_d)
    k.dma("gpsimd", W["ones1"][:], ones_d)
    W["sb1"] = k.sbuf("m_sb1_sb", [128, 8], F32)
    W["sb2"] = k.sbuf("m_sb2_sb", [128, 8], F32)
    W["eb1"] = k.sbuf("m_eb1_sb", [128, 120], F32)
    k.dma("sync", W["sb1"][:], sb1_d)
    k.dma("sync", W["sb2"][:], sb2_d)
    k.dma("sync", W["eb1"][:], eb1_d)
    k.barrier()
    return W


def emit_moe(k, C, x_in, x_out, W, coef=0.5, nt=NT, dbg=False):
    u = k.uid()
    idf = C["idf"]
    kind = "ExternalOutput" if dbg else "Internal"
    xnT_d = k.dram(f"moe_xnT{u}", [1024, T], BF16, kind=kind)
    cwb_d = k.dram(f"moe_cwb{u}", [15, T], BF16, kind=kind)

    emit_ln(k, C, x_in, xnT_d, nt=nt)

    mouter = k.scope()
    mouter.__enter__()
    cwT = k.sbuf("m_cwT", [15, T], BF16)

    with k.scope():
        xs = [k.sbuf("m_xs", [128, 8, 512], BF16) for _ in range(2)]
        rw_sb = k.sbuf("m_rw", [128, 8, 15], BF16)
        lg = k.sbuf("m_lg", [128, 15], F32)
        mx = k.sbuf("m_mx", [128, 1], F32)
        sm = k.sbuf("m_sm", [128, 1], F32)
        pd = k.sbuf("m_pd", [128, 30], F32)
        rk = k.sbuf("m_rk", [128, 15], F32)
        cwf = k.sbuf("m_cwf", [128, 15], F32)
        cmp_sb = k.sbuf("m_cmp", [128, 14, 15], F32)
        cwb16 = k.sbuf("m_cw16", [128, 15], BF16)
        cwrow = k.sbuf("m_cwrow", [15, 128], BF16)
        pR = k.psum("m_pR", [128, 15], F32)
        pX = k.psum("m_pX", [15, 128], BF16)

        k.dma("gpsimd", rw_sb[:],
              W["rW"].rearrange("(kb p) o -> p kb o", p=128))
        wv = k.cnt["d_gpsimd"]
        srcr = xnT_d.rearrange("(kb p) t -> p kb t", p=128)
        xsfree = {0: 0, 1: 0}
        cwdma = 0
        for ti in range(nt):
            b = ti % 2
            ld = k.dma("sync", xs[b][:], srcr[:, :, ti * 512:(ti + 1) * 512],
                       waits=[("T", xsfree[b])])
            for c4 in range(4):
                for kb in range(8):
                    wl = [("d_sync", ld), ("d_gpsimd", wv),
                          ("V", k.cnt["V"])] if kb == 0 else []
                    k.op("tensor", lambda e, kb=kb, c4=c4, b=b: e.matmul(
                        pR[:], xs[b][:, kb, c4 * 128:(c4 + 1) * 128],
                        rw_sb[:, kb, :], start=(kb == 0), stop=False),
                        waits=wl)
                mmv = k.op("tensor", lambda e: e.matmul(
                    pR[:], W["ones1"][:], W["rb"][:], start=False, stop=True),
                    inc="T")
                h0 = k.op("vector", lambda e: e.tensor_copy(lg[:], pR[:]),
                          waits=[("T", mmv)], inc="V")
                vmx = k.op("vector", lambda e: e.tensor_reduce(
                    mx[:], lg[:], AX.X, ALU.max, negate=True),
                    waits=[("V", h0)], inc="V")
                sv = k.op("scalar", lambda e: e.activation(
                    lg[:], lg[:], AF.Exp, bias=mx[:, 0:1]), inc="S",
                    waits=[("V", vmx)])
                h1 = k.op("vector", lambda e: e.tensor_reduce(
                    sm[:], lg[:], AX.X, ALU.add), waits=[("S", sv)], inc="V")
                h2 = k.op("vector", lambda e: e.reciprocal(sm[:], sm[:]),
                          waits=[("V", h1)], inc="V")
                h3 = k.op("vector", lambda e: e.tensor_scalar(
                    lg[:], lg[:], sm[:, 0:1], None, ALU.mult),
                    waits=[("V", h2)], inc="V")
                h4 = k.op("vector", lambda e: e.tensor_copy(pd[:, 0:15], lg[:]),
                          waits=[("V", h3)], inc="V")
                h5 = k.op("vector", lambda e: e.tensor_copy(pd[:, 15:30], lg[:]),
                          waits=[("V", h4)], inc="V")
                h6 = 0
                for dl in range(1, 15):
                    h6 = k.op("vector", lambda e, dl=dl: e.tensor_tensor(
                        cmp_sb[:, dl - 1, :], pd[:, dl:dl + 15], lg[:],
                        ALU.is_gt), waits=[("V", h5)] if dl == 1 else (),
                        inc="V" if dl == 14 else None)
                cmpv = bass.AP(cmp_sb.ap().tensor, 0, [[210, 128], [1, 15], [15, 14]])
                h7 = k.op("vector", lambda e, cmpv=cmpv: e.tensor_reduce(
                    rk[:], cmpv, AX.X, ALU.add), waits=[("V", h6)], inc="V")
                h8 = k.op("vector", lambda e: e.tensor_scalar(
                    rk[:], rk[:], 3.0, None, ALU.is_lt), waits=[("V", h7)],
                    inc="V")
                h9 = k.op("vector", lambda e: e.tensor_tensor(
                    cwf[:], lg[:], rk[:], ALU.mult), waits=[("V", h8)],
                    inc="V")
                vv = k.op("vector", lambda e: e.tensor_copy(cwb16[:], cwf[:]),
                          waits=[("V", h9)], inc="V")
                tv = k.op("tensor", lambda e: e.transpose(
                    pX[:], cwb16[:], C["idb"][:]), waits=[("V", vv)], inc="T")
                vv = k.op("vector", lambda e: e.tensor_copy(cwrow[:], pX[:]),
                          waits=[("T", tv)], inc="V")
                vv = k.op("vector", lambda e, ti=ti, c4=c4: e.tensor_copy(
                    cwT[:, ti * 512 + c4 * 128: ti * 512 + (c4 + 1) * 128],
                    cwrow[:]), inc="V")
            xsfree[b] = k.cnt["T"]
        cwdma = k.dma("sync", cwb_d, cwT[:], waits=[("V", k.cnt["V"])])
        k.barrier()

    BLK = 1024
    nblk = (nt * 512) // BLK
    with k.scope():
        xnb = k.sbuf("m_xnb", [128, 8, BLK], BF16)
        routed = k.sbuf("m_routed", [128, 8, BLK], BF16)
        sw1_sb = k.sbuf("m_sw1", [128, 8, 1024], BF16)
        sw2_sb = k.sbuf("m_sw2", [128, 8, 1024], BF16)
        ew1_sb = [k.sbuf("m_ew1", [128, 8, 1024], BF16) for _ in range(2)]
        ew2_sb = [k.sbuf("m_ew2", [128, 8, 1024], BF16) for _ in range(2)]
        cwt_sb = [k.sbuf("m_cwt", [128, BLK], BF16) for _ in range(2)]
        ht = k.sbuf("m_ht", [128, 8, 512], BF16)
        hs = k.sbuf("m_hs", [128, 8, 512], BF16)
        tmp = k.sbuf("m_tmp", [128, 512], BF16)
        tmp2 = [k.sbuf("m_tmp2", [128, 512], BF16) for _ in range(2)]
        tmpf = k.sbuf("m_tmpf", [128, 512], F32)
        xt = [k.sbuf("m_xt", [128, 4, 1024], F32) for _ in range(2)]
        psH = [k.psum("m_psH", [128, 512], F32) for _ in range(3)]
        psR = [k.psum("m_psR", [128, 512], F32) for _ in range(3)]
        TF = [k.psum("m_TF", [128, 128], F32) for _ in range(2)]

        k.dma("gpsimd", sw1_sb[:],
              W["sW1"].rearrange("(kb p) o -> p kb o", p=128))
        swv = k.dma("gpsimd", sw2_sb[:],
                    W["sW2"].rearrange("(kb p) o -> p kb o", p=128))
        xr = x_in.rearrange("(ti s p) d -> ti p s d", s=4, p=128)
        xo = x_out.rearrange("(ti s p) d -> ti p s d", s=4, p=128)
        srcr = xnT_d.rearrange("(kb p) t -> p kb t", p=128)
        ew1r = W["eW1"].rearrange("E (kb p) o -> E p kb o", p=128)
        ew2r = W["eW2"].rearrange("E (kb p) o -> E p kb o", p=128)

        hfree = {i: ("S", 0) for i in range(3)}
        rfree = {i: ("S", 0) for i in range(3)}
        tmpfree2 = {0: ("V", 0), 1: ("V", 0)}
        lastadd = {i: ("V", 0) for i in range(8)}
        tffree = {0: ("V", 0), 1: ("V", 0)}
        ewfree = {0: 0, 1: 0}
        cwfree = {0: 0, 1: 0}
        xnbfree = 0
        xdma = {0: 0, 1: 0}
        for blk in range(nblk):
            t0 = blk * BLK
            ldx = k.dma("sync", xnb[:], srcr[:, :, t0:t0 + BLK],
                        waits=[("T", xnbfree)])
            for ex in range(15):
                eb = ex % 2
                lw1 = k.dma("gpsimd", ew1_sb[eb][:], ew1r[ex],
                            waits=[("T", ewfree[eb])])
                lw2 = k.dma("gpsimd", ew2_sb[eb][:], ew2r[ex])
                bc = k.dma("scalar", cwt_sb[eb][:],
                           bass.AP(cwb_d.tensor, ex * T + t0,
                                   [[0, 128], [1, BLK]]),
                           waits=[("V", cwfree[eb]), ("d_sync", cwdma)])
                for sub in range(BLK // 512):
                    ts0 = sub * 512
                    for ob in range(8):
                        pb = ob % 3
                        for kb in range(8):
                            wl = []
                            if kb == 0:
                                wl = [("d_sync", ldx), ("d_gpsimd", lw1),
                                      hfree[pb]]
                            mmv = k.op("tensor", lambda e_, kb=kb, ob=ob,
                                       eb=eb, ts0=ts0, pb=pb: e_.matmul(
                                           psH[pb][:],
                                           ew1_sb[eb][:, kb, ob * 128:(ob + 1) * 128],
                                           xnb[:, kb, ts0:ts0 + 512],
                                           start=(kb == 0), stop=(kb == 7)),
                                       waits=wl,
                                       inc="T" if kb == 7 else None)
                        hv = k.op("scalar", lambda e_, ob=ob, ex=ex, pb=pb:
                                  e_.activation(
                                      ht[:, ob, :], psH[pb][:], AF.Relu,
                                      bias=W["eb1"][:, ex * 8 + ob:
                                                    ex * 8 + ob + 1]),
                                  waits=[("T", mmv)], inc="S")
                        hfree[pb] = ("S", hv)
                    cw_a = bass.AP(cwt_sb[eb].ap().tensor, ts0,
                                   [[BLK, 128], [0, 8], [1, 512]])
                    hv = k.op("vector", lambda e_, cw_a=cw_a: e_.tensor_tensor(
                        ht[:], ht[:], cw_a, ALU.mult),
                        waits=[("d_scalar", bc), ("S", hv)], inc="V")
                    for ob in range(8):
                        pb = ob % 3
                        for kb in range(8):
                            wl = []
                            if kb == 0:
                                wl = [("d_gpsimd", lw2), rfree[pb], ("V", hv)]
                            mmv = k.op("tensor", lambda e_, kb=kb, ob=ob,
                                       eb=eb, pb=pb: e_.matmul(
                                           psR[pb][:],
                                           ew2_sb[eb][:, kb, ob * 128:(ob + 1) * 128],
                                           ht[:, kb, :],
                                           start=(kb == 0), stop=(kb == 7)),
                                       waits=wl,
                                       inc="T" if kb == 7 else None)
                        if ex == 0:
                            rv = k.op("scalar", lambda e_, ob=ob, ts0=ts0,
                                      pb=pb: e_.copy(
                                          routed[:, ob, ts0:ts0 + 512],
                                          psR[pb][:]),
                                      waits=[("T", mmv)], inc="S")
                            rfree[pb] = ("S", rv)
                            lastadd[ob] = ("S", rv)
                        else:
                            rv = k.op("scalar", lambda e_, pb=pb, tb=ob % 2:
                                      e_.copy(tmp2[tb][:], psR[pb][:]),
                                      waits=[("T", mmv),
                                             tmpfree2[ob % 2]], inc="S")
                            av2 = k.op("vector", lambda e_, ob=ob, ts0=ts0,
                                       tb=ob % 2: e_.tensor_tensor(
                                           routed[:, ob, ts0:ts0 + 512],
                                           routed[:, ob, ts0:ts0 + 512],
                                           tmp2[tb][:], ALU.add),
                                       waits=[("S", rv), lastadd[ob]],
                                       inc="V")
                            tmpfree2[ob % 2] = ("V", av2)
                            lastadd[ob] = ("V", av2)
                            rfree[pb] = ("S", rv)
                cwfree[eb] = k.cnt["V"]
                ewfree[eb] = k.cnt["T"]
            for sub in range(BLK // 512):
                ts0 = sub * 512
                ti = (t0 + sub * 512) // 512
                b = ti % 2
                ldt = k.dma("sync", xt[b][:], xr[ti],
                            waits=[("d_sync", xdma[b])])
                for ob in range(8):
                    pb = ob % 2
                    for kb in range(8):
                        wl = [("d_gpsimd", swv), hfree[pb]] if kb == 0 else []
                        mmv = k.op("tensor", lambda e_, kb=kb, ob=ob, ts0=ts0,
                                   pb=pb: e_.matmul(
                                       psH[pb][:],
                                       sw1_sb[:, kb, ob * 128:(ob + 1) * 128],
                                       xnb[:, kb, ts0:ts0 + 512],
                                       start=(kb == 0), stop=(kb == 7)),
                                   waits=wl, inc="T" if kb == 7 else None)
                    hv = k.op("vector", lambda e_, ob=ob, pb=pb: e_.tensor_scalar(
                        hs[:, ob, :], psH[pb][:], W["sb1"][:, ob:ob + 1], 0.0,
                        ALU.add, ALU.max), waits=[("T", mmv)], inc="V")
                    hfree[pb] = ("V", hv)
                for ob in range(8):
                    pb = ob % 2
                    for kb in range(8):
                        wl = [rfree[pb], ("V", hv)] if kb == 0 else []
                        k.op("tensor", lambda e_, kb=kb, ob=ob, pb=pb:
                             e_.matmul(
                                 psR[pb][:],
                                 sw2_sb[:, kb, ob * 128:(ob + 1) * 128],
                                 hs[:, kb, :], start=(kb == 0), stop=False),
                             waits=wl)
                    mmv = k.op("tensor", lambda e_, ob=ob, ti=ti, pb=pb:
                               e_.matmul(
                                   psR[pb][:],
                                   W["eb2"][:, ob * 128:(ob + 1) * 128],
                                   cwT[:, ti * 512:(ti + 1) * 512],
                                   start=False, stop=True), inc="T")
                    v1 = k.op("vector", lambda e_, ob=ob, pb=pb: e_.tensor_scalar(
                        tmpf[:], psR[pb][:], W["sb2"][:, ob:ob + 1], coef,
                        ALU.add, ALU.mult), waits=[("T", mmv)], inc="V")
                    v1 = k.op("vector", lambda e_, ob=ob, ts0=ts0:
                              e_.scalar_tensor_tensor(
                                  tmpf[:], routed[:, ob, ts0:ts0 + 512], coef,
                                  tmpf[:], ALU.mult, ALU.add),
                              waits=[("V", v1)], inc="V")
                    rfree[pb] = ("V", v1)
                    av = 0
                    for s in range(4):
                        tb = s % 2
                        tv = k.op("tensor", lambda e_, s=s, tb=tb: e_.transpose(
                            TF[tb][:], tmpf[:, s * 128:(s + 1) * 128], idf[:]),
                            waits=[("V", v1), tffree[tb]], inc="T")
                        av = k.op("vector", lambda e_, s=s, ob=ob, b=b, tb=tb:
                                  e_.tensor_tensor(
                                      xt[b][:, s, ob * 128:(ob + 1) * 128],
                                      xt[b][:, s, ob * 128:(ob + 1) * 128],
                                      TF[tb][:], ALU.add),
                                  waits=[("T", tv)] + ([("d_sync", ldt)]
                                                       if (s == 0 and ob == 0)
                                                       else []), inc="V")
                        tffree[tb] = ("V", av)
                xdma[b] = k.dma("sync", xo[ti], xt[b][:], waits=[("V", av)])
            xnbfree = k.cnt["T"]
    mouter.__exit__(None, None, None)
    k.barrier()


def build_launch_moe(coef=0.5, dbg=False):
    k = K()
    x_in = k.dram("x_in", [T, D], F32, kind="ExternalInput")
    x_out = k.dram("x_out", [T, D], F32, kind="ExternalOutput")
    C = setup_consts(k)
    W = declare_moe_weights(k, C)
    emit_moe(k, C, x_in, x_out, W, coef=coef, dbg=dbg)
    return k.finalize()


def run_launch_moe(x_full, ins, n_cores=8):
    wp = prep_weights_moe(ins)
    nc = build_launch_moe()
    shared = base_inputs()
    shared.update(wp)
    maps = []
    for c in range(n_cores):
        m = dict(shared)
        m["x_in"] = np.ascontiguousarray(x_full[c])
        maps.append(m)
    res = run_bass_kernel_spmd(nc, maps, core_ids=list(range(n_cores)))
    return np.stack([np.asarray(r["x_out"]) for r in res.results])




def prep_weights_nys(ins):
    f = np.float32
    ct = np.ascontiguousarray
    out = {}
    out["n_Wq"] = ct(ins["L0_attn_Wq"].astype(f))
    out["n_Wk"] = ct(ins["L0_attn_Wk"].astype(f))
    out["n_Wv"] = ct(ins["L0_attn_Wv"].astype(f))
    out["n_bq"] = ct(ins["L0_attn_bq"].astype(f).reshape(8, 128).T)
    out["n_bk"] = ct(ins["L0_attn_bk"].astype(f).reshape(8, 128).T)
    out["n_bv"] = ct(ins["L0_attn_bv"].astype(f).reshape(8, 128).T)
    w = ins["L0_attn_conv_w"].astype(f).reshape(16, 65)
    B = np.zeros((16, 3, 128, 128), f)
    for r in (-1, 0, 1):
        for i in range(128):
            for j in range(128):
                idx = i - j + r * 128 + 32
                if 0 <= idx < 65:
                    B[:, r + 1, i, j] = w[:, idx]
    out["n_band"] = ct(B)
    return out


def declare_nys_weights(k, C):
    W = {}
    W["Wq"] = k.dram("n_Wq", [1024, 1024], F32, kind="ExternalInput")
    W["Wk"] = k.dram("n_Wk", [1024, 1024], F32, kind="ExternalInput")
    W["Wv"] = k.dram("n_Wv", [1024, 1024], F32, kind="ExternalInput")
    for nm in ("bq", "bk", "bv"):
        d = k.dram("n_" + nm, [128, 8], F32, kind="ExternalInput")
        t = k.sbuf("n_" + nm + "_sb", [128, 8], F32)
        k.dma("sync", t[:], d)
        W[nm] = t
    W["band_d"] = k.dram("n_band", [16, 3, 128, 128], F32,
                         kind="ExternalInput")
    k.barrier()
    return W


def emit_proj(k, C, xnT_d, wT_d, dst_d, bias, scale=None, nt=NT):
    with k.scope():
        w_sb = k.sbuf("pj_w", [128, 8, 1024], BF16)
        rhs = [k.sbuf("pj_rhs", [128, 8, 512], BF16) for _ in range(2)]
        obuf = [k.sbuf("pj_o", [128, 8, 512], BF16) for _ in range(2)]
        ps = [k.psum("pj_ps", [128, 512], F32) for _ in range(4)]
        bsc = k.sbuf("pj_bsc", [128, 8], F32)
        dvb = k.cnt["d_sync"]
        if scale is None:
            k.op("vector", lambda e: e.tensor_copy(bsc[:], bias[:]),
                 waits=[("d_sync", dvb)], inc="V")
        else:
            k.op("vector", lambda e: e.tensor_scalar(
                bsc[:], bias[:], scale, None, ALU.mult),
                waits=[("d_sync", dvb)], inc="V")
        wv = k.dma("gpsimd", w_sb[:],
                   wT_d.rearrange("(kb p) o -> p kb o", p=128))
        srcr = xnT_d.rearrange("(kb p) t -> p kb t", p=128)
        dstr = dst_d.rearrange("(ob p) t -> p ob t", p=128)
        bankfree = {i: ("S", 0) for i in range(4)}
        rhsfree = {0: 0, 1: 0}
        odma = {0: 0, 1: 0}
        scl = 1.0 if scale is None else scale
        for ti in range(nt):
            b = ti % 2
            ld = k.dma("sync", rhs[b][:], srcr[:, :, ti * 512:(ti + 1) * 512],
                       waits=[("T", rhsfree[b])])
            vv = 0
            for ob in range(8):
                pb = ob % 4
                for kb in range(8):
                    wl = []
                    if kb == 0:
                        wl = [("d_sync", ld), ("d_gpsimd", wv), bankfree[pb],
                              ("V", k.cnt["V"])]
                    mmv = k.op("tensor", lambda e, kb=kb, ob=ob, b=b, pb=pb:
                               e.matmul(ps[pb][:],
                                        w_sb[:, kb, ob * 128:(ob + 1) * 128],
                                        rhs[b][:, kb, :], start=(kb == 0),
                                        stop=(kb == 7)),
                               waits=wl, inc="T" if kb == 7 else None)
                vv = k.op("scalar", lambda e, ob=ob, b=b, pb=pb, scl=scl:
                          e.activation(
                              obuf[b][:, ob, :], ps[pb][:], AF.Identity,
                              bias=bsc[:, ob:ob + 1], scale=scl),
                          waits=[("T", mmv)] + ([("d_sync", odma[b])]
                                                if ob == 0 else []),
                          inc="S")
                bankfree[pb] = ("S", vv)
            odma[b] = k.dma("sync", dstr[:, :, ti * 512:(ti + 1) * 512],
                            obuf[b][:], waits=[("S", vv)])
            rhsfree[b] = k.cnt["T"]
    k.barrier()


def emit_nystrom(k, C, x_in, x_out, W, nt=NT):
    u = k.uid()
    idb = C["idb"]
    idf = C["idf"]
    Tl = nt * 512
    NL = Tl // 128
    NCH = Tl // 128
    xnT_d = k.dram(f"ny_xnT{u}", [1024, T], BF16)
    qd = k.dram(f"ny_q{u}", [1024, T], BF16)
    kd = k.dram(f"ny_k{u}", [1024, T], BF16)
    vd = k.dram(f"ny_v{u}", [1024, T], BF16)
    aod = k.dram(f"ny_ao{u}", [1024, T], BF16)

    emit_ln(k, C, x_in, xnT_d, nt=nt)
    SC = 0.3535533905932738
    emit_proj(k, C, xnT_d, W["Wq"], qd, W["bq"], scale=SC, nt=nt)
    emit_proj(k, C, xnT_d, W["Wk"], kd, W["bk"], scale=SC, nt=nt)
    emit_proj(k, C, xnT_d, W["Wv"], vd, W["bv"], scale=None, nt=nt)

    with k.scope():
        W = dict(W)
        W["band"] = k.sbuf("n_band_sb", [128, 48, 128], BF16)
        bdv = k.dma("gpsimd", W["band"][:],
                    W["band_d"].rearrange("h r i j -> i (h r) j"))
        q2 = k.sbuf("ny_q2", [128, Tl], BF16)
        k2t = k.sbuf("ny_k2t", [128, Tl], BF16)
        v2 = k.sbuf("ny_v2", [128, Tl], BF16)
        lm = k.sbuf("ny_lm", [128, 128], F32)
        lmb = k.sbuf("ny_lmb", [128, 128], BF16)
        vt = k.sbuf("ny_vt", [128, NCH, 64], BF16)
        k3t = k.sbuf("ny_k3t", [128, NCH, 64], BF16)
        k1t = k.sbuf("ny_k1t", [64, Tl], BF16)
        atT = k.sbuf("ny_atT", [64, Tl], BF16)
        ctb = k.sbuf("ny_ctb", [128, Tl], BF16)
        k3f = k.sbuf("ny_k3f", [64, Tl], F32)
        k3nb = k.sbuf("ny_k3nb", [64, Tl], BF16)
        k3s = k.sbuf("ny_k3s", [64, 4], F32)
        k1f = k.sbuf("ny_k1f", [128, 4, 64], F32)
        k1n = k.sbuf("ny_k1n", [128, 4, 64], BF16)
        k1s = k.sbuf("ny_k1s", [128, 12], F32)
        iv = k.sbuf("ny_iv", [64, 13, 64], BF16)
        ivf = k.sbuf("ny_ivf", [64, 8], F32)
        pA = k.psum("ny_pA", [128, 512], F32)
        pS = k.psum("ny_pS", [64, 64], F32)
        pS2 = k.psum("ny_pS2", [64, 64], F32)
        pC = [k.psum("ny_pC", [64, 512], F32) for _ in range(2)]
        pT = [k.psum("ny_pT", [128, 128], BF16) for _ in range(2)]

        qr = qd.rearrange("(hp p) t -> hp p t", p=128)
        kr = kd.rearrange("(hp p) t -> hp p t", p=128)
        vr = vd.rearrange("(hp p) t -> hp p t", p=128)
        aor = aod.rearrange("(hp p) t -> hp p t", p=128)

        ctbdma = 0
        for hp in range(8):
            pv = k.cnt.get("d_sync", 0)
            l1 = k.dma("sync", q2[:], qr[hp], waits=[("T", k.cnt["T"]),
                                                     ("V", k.cnt["V"]),
                                                     ("d_gpsimd", bdv)])
            l2 = k.dma("sync", v2[:], vr[hp])
            l3 = k.dma("sync", k2t[:], kr[hp])
            qv = bass.AP(q2.ap().tensor, 0, [[Tl, 128], [128, NL], [1, 128]])
            kv = bass.AP(k2t.ap().tensor, 0, [[Tl, 128], [128, NL], [1, 128]])
            r1 = k.op("vector", lambda e, qv=qv: e.tensor_reduce(
                lm[:, 0:NL], qv, AX.X, ALU.add), waits=[("d_sync", l1)],
                inc="V")
            r2 = k.op("vector", lambda e, kv=kv: e.tensor_reduce(
                lm[:, 64:64 + NL], kv, AX.X, ALU.add),
                waits=[("d_sync", l3)], inc="V")
            r3 = k.op("vector", lambda e: e.tensor_scalar(
                lmb[:], lm[:], 1.0 / 128.0, None, ALU.mult),
                waits=[("V", r1), ("V", r2)], inc="V")

            for hh in range(2):
                P = hh * 64
                ql = lmb[P:P + 64, 0:NL]
                kl = lmb[P:P + 64, 64:64 + NL]
                qh = q2[P:P + 64, :]
                kh = k2t[P:P + 64, :]
                vh = v2[P:P + 64, :]

                m2 = k.op("tensor", lambda e, ql=ql, kl=kl: e.matmul(
                    pS[:], ql, kl, start=True, stop=True),
                    waits=[("V", k.cnt["V"])], inc="T")
                c0 = k.op("vector", lambda e: e.tensor_copy(
                    ivf[:, 0:1], pS[:, 0:1]), waits=[("T", m2)], inc="V")
                c0 = k.op("vector", lambda e: e.tensor_reduce(
                    ivf[:, 0:1], pS[:], AX.X, ALU.max, negate=True),
                    waits=[("T", m2), ("V", c0)], inc="V")
                sv = k.op("scalar", lambda e: e.activation(
                    iv[:, 12, :], pS[:], AF.Exp, bias=ivf[:, 0:1]),
                    waits=[("V", c0)], inc="S")
                c1 = k.op("vector", lambda e: e.tensor_reduce(
                    ivf[:, 1:2], iv[:, 12, :], AX.X, ALU.add),
                    waits=[("S", sv)], inc="V")
                c2 = k.op("vector", lambda e: e.reciprocal(
                    ivf[:, 1:2], ivf[:, 1:2]), waits=[("V", c1)], inc="V")
                c3 = k.op("vector", lambda e: e.tensor_scalar(
                    iv[:, 0, :], iv[:, 12, :], ivf[:, 1:2], None, ALU.mult),
                    waits=[("V", c2)], inc="V")
                t0 = k.op("tensor", lambda e: e.transpose(
                    pT[0][0:64, 0:64], iv[:, 0, :], idb[0:64, 0:64]),
                    waits=[("V", c3)], inc="T")
                c4 = k.op("vector", lambda e: e.tensor_copy(
                    iv[:, 1, :], pT[0][0:64, 0:64]), waits=[("T", t0)],
                    inc="V")
                m3 = k.op("tensor", lambda e: e.matmul(
                    pS2[0:1, :], C["onesb"][0:64, 0:1], iv[:, 0, :],
                    start=True, stop=True), waits=[("V", c3)], inc="T")
                c5 = k.op("vector", lambda e: e.tensor_reduce(
                    ivf[0:1, 2:3], pS2[0:1, :], AX.X, ALU.max),
                    waits=[("T", m3)], inc="V")
                c6 = k.op("vector", lambda e: e.reciprocal(
                    ivf[0:1, 2:3], ivf[0:1, 2:3]), waits=[("V", c5)], inc="V")
                c7 = k.op("vector", lambda e: e.tensor_copy(
                    iv[0:1, 12, 0:1], ivf[0:1, 2:3]), waits=[("V", c6)],
                    inc="V")
                m4 = k.op("tensor", lambda e: e.matmul(
                    pS2[:, 0:1], C["onesb"][0:1, 0:64], iv[0:1, 12, 0:1],
                    start=True, stop=True), waits=[("V", c7)], inc="T")
                c8 = k.op("vector", lambda e: e.tensor_copy(
                    ivf[:, 3:4], pS2[:, 0:1]), waits=[("T", m4)], inc="V")
                c9 = k.op("vector", lambda e: e.tensor_scalar(
                    iv[:, 2, :], iv[:, 1, :], ivf[:, 3:4], None, ALU.mult),
                    waits=[("V", c8)], inc="V")
                c10 = k.op("vector", lambda e: e.tensor_scalar(
                    iv[:, 3, :], iv[:, 0, :], ivf[:, 3:4], None, ALU.mult),
                    inc="V")
                vvv = c10
                for it in range(6):
                    m5 = k.op("tensor", lambda e: e.matmul(
                        pS[:], iv[:, 1, :], iv[:, 2, :], start=True,
                        stop=True), waits=[("V", vvv)], inc="T")
                    m6 = k.op("tensor", lambda e: e.matmul(
                        pS2[:], iv[:, 2, :], iv[:, 1, :], start=True,
                        stop=True), inc="T")
                    d1 = k.op("vector", lambda e: e.tensor_copy(
                        iv[:, 4, :], pS[:]), waits=[("T", m6)], inc="V")
                    d2 = k.op("vector", lambda e: e.tensor_copy(
                        iv[:, 5, :], pS2[:]), waits=[("V", d1)], inc="V")
                    d3 = k.op("vector", lambda e: e.tensor_tensor(
                        iv[:, 6, :], C["i7"][:], iv[:, 4, :],
                        ALU.subtract), waits=[("V", d2)], inc="V")
                    m7 = k.op("tensor", lambda e: e.matmul(
                        pS[:], iv[:, 5, :], iv[:, 6, :], start=True,
                        stop=True), waits=[("V", d3)], inc="T")
                    d4 = k.op("vector", lambda e: e.tensor_copy(
                        iv[:, 7, :], pS[:]), waits=[("T", m7)], inc="V")
                    d5 = k.op("vector", lambda e: e.tensor_tensor(
                        iv[:, 8, :], C["i15"][:], iv[:, 7, :],
                        ALU.subtract), waits=[("V", d4)], inc="V")
                    m8 = k.op("tensor", lambda e: e.matmul(
                        pS[:], iv[:, 5, :], iv[:, 8, :], start=True,
                        stop=True), waits=[("V", d5)], inc="T")
                    d6 = k.op("vector", lambda e: e.tensor_copy(
                        iv[:, 9, :], pS[:]), waits=[("T", m8)], inc="V")
                    d7 = k.op("vector", lambda e: e.tensor_tensor(
                        iv[:, 10, :], C["i13"][:], iv[:, 9, :],
                        ALU.subtract), waits=[("V", d6)], inc="V")
                    m9 = k.op("tensor", lambda e: e.matmul(
                        pS[:], iv[:, 3, :], iv[:, 10, :], start=True,
                        stop=True), waits=[("V", d7)], inc="T")
                    m10 = k.op("tensor", lambda e: e.matmul(
                        pS2[:], iv[:, 10, :], iv[:, 3, :], start=True,
                        stop=True), inc="T")
                    d8 = k.op("vector", lambda e: e.tensor_scalar(
                        iv[:, 2, :], pS[:], 0.25, None, ALU.mult),
                        waits=[("T", m10)], inc="V")
                    vvv = k.op("vector", lambda e: e.tensor_scalar(
                        iv[:, 3, :], pS2[:], 0.25, None, ALU.mult),
                        waits=[("V", d8)], inc="V")

                for g4 in range(NCH // 4):
                    mm = 0
                    for cc in range(4):
                        ch = g4 * 4 + cc
                        mm = k.op("tensor", lambda e, ch=ch, cc=cc, qh=qh,
                                  kl=kl: e.matmul(
                                      pA[:, cc * 64:cc * 64 + 64],
                                      qh[:, ch * 128:(ch + 1) * 128], kl,
                                      start=True, stop=True),
                                  waits=[("V", k.cnt["V"])] if cc == 0 else (),
                                  inc="T" if cc == 3 else None)
                    e0 = k.op("vector", lambda e: e.tensor_copy(
                        k1f[:], pA[:, 0:256].rearrange("p (c m) -> p c m", m=64)),
                        waits=[("T", mm)], inc="V")
                    e1 = k.op("vector", lambda e: e.tensor_reduce(
                        k1s[:, 0:4], k1f[:], AX.X, ALU.max, negate=True),
                        waits=[("V", e0)], inc="V")
                    sv2 = 0
                    for cc in range(4):
                        sv2 = k.op("scalar", lambda e, cc=cc: e.activation(
                            k1f[:, cc, :], k1f[:, cc, :], AF.Exp,
                            bias=k1s[:, cc:cc + 1]),
                            waits=[("V", e1)], inc="S")
                    e2 = k.op("vector", lambda e: e.tensor_reduce(
                        k1s[:, 4:8], k1f[:], AX.X, ALU.add),
                        waits=[("S", sv2)], inc="V")
                    e3 = k.op("vector", lambda e: e.reciprocal(
                        k1s[:, 4:8], k1s[:, 4:8]), waits=[("V", e2)], inc="V")
                    e4 = 0
                    for cc in range(4):
                        e4 = k.op("vector", lambda e, cc=cc: e.tensor_scalar(
                            k1n[:, cc, :], k1f[:, cc, :], k1s[:, 4 + cc:5 + cc],
                            None, ALU.mult), waits=[("V", e3)], inc="V")
                    for cc in range(4):
                        ch = g4 * 4 + cc
                        tb = cc % 2
                        tv = k.op("tensor", lambda e, cc=cc, tb=tb: e.transpose(
                            pT[tb][0:64, 0:128], k1n[:, cc, :], idb[:]),
                            waits=[("V", k.cnt["V"])], inc="T")
                        k.op("vector", lambda e, ch=ch, tb=tb: e.tensor_copy(
                            k1t[:, ch * 128:(ch + 1) * 128],
                            pT[tb][0:64, 0:128]), waits=[("T", tv)], inc="V")

                mmv3 = 0
                for ch in range(Tl // 512):
                    cb = ch % 2
                    mmv3 = k.op("tensor", lambda e, ch=ch, cb=cb, ql=ql,
                                kh=kh: e.matmul(
                                    pC[cb][:], ql,
                                    kh[:, ch * 512:(ch + 1) * 512],
                                    start=True, stop=True),
                                waits=[("V", k.cnt["V"]),
                                       ("S", k.cnt.get("S", 0))], inc="T")
                    k.op("scalar", lambda e, ch=ch, cb=cb: e.copy(
                        k3f[:, ch * 512:(ch + 1) * 512], pC[cb][:]),
                        waits=[("T", mmv3)], inc="S")
                f1 = k.op("vector", lambda e: e.tensor_reduce(
                    k3s[:, 0:1], k3f[:], AX.X, ALU.max, negate=True),
                    waits=[("V", k.cnt["V"]), ("S", k.cnt.get("S", 0))],
                    inc="V")
                sv3 = k.op("scalar", lambda e: e.activation(
                    k3f[:], k3f[:], AF.Exp, bias=k3s[:, 0:1]),
                    waits=[("V", f1)], inc="S")
                f2 = k.op("vector", lambda e: e.tensor_reduce(
                    k3s[:, 1:2], k3f[:], AX.X, ALU.add), waits=[("S", sv3)],
                    inc="V")
                f3 = k.op("vector", lambda e: e.reciprocal(
                    k3s[:, 1:2], k3s[:, 1:2]), waits=[("V", f2)], inc="V")
                f4 = k.op("vector", lambda e: e.tensor_scalar(
                    k3nb[:], k3f[:], k3s[:, 1:2], None, ALU.mult),
                    waits=[("V", f3)], inc="V")
                for ch in range(NCH):
                    tb = ch % 2
                    tv = k.op("tensor", lambda e, ch=ch, tb=tb: e.transpose(
                        pT[tb][:, 0:64], k3nb[:, ch * 128:(ch + 1) * 128],
                        idb[0:64, 0:64]), waits=[("V", k.cnt["V"])], inc="T")
                    k.op("vector", lambda e, ch=ch, tb=tb: e.tensor_copy(
                        k3t[:, ch, :], pT[tb][:, 0:64]),
                        waits=[("T", tv)], inc="V")
                    tv2 = k.op("tensor", lambda e, ch=ch, tb=tb, vh=vh,
                               P=P: e.transpose(
                                   pT[tb][:, 64:128],
                                   vh[:, ch * 128:(ch + 1) * 128],
                                   idb[P:P + 64, P:P + 64]),
                               waits=[("V", k.cnt["V"]), ("d_sync", l2)],
                               inc="T")
                    k.op("vector", lambda e, ch=ch, tb=tb: e.tensor_copy(
                        vt[:, ch, :], pT[tb][:, 64:128]),
                        waits=[("T", tv2)], inc="V")

                for ch in range(NCH):
                    mv = k.op("tensor", lambda e, ch=ch: e.matmul(
                        pS[:], k3t[:, ch, :], vt[:, ch, :],
                        start=(ch == 0), stop=(ch == NCH - 1)),
                        waits=[("V", k.cnt["V"])] if ch == 0 else (),
                        inc="T" if ch == NCH - 1 else None)
                g1 = k.op("vector", lambda e: e.tensor_copy(
                    iv[:, 11, :], pS[:]), waits=[("T", mv)], inc="V")

                for ch in range(Tl // 512):
                    cb = ch % 2
                    ma = k.op("tensor", lambda e, ch=ch, cb=cb: e.matmul(
                        pC[cb][:], iv[:, 2, :],
                        k1t[:, ch * 512:(ch + 1) * 512],
                        start=True, stop=True),
                        waits=[("V", k.cnt["V"]),
                               ("S", k.cnt.get("S", 0))], inc="T")
                    k.op("scalar", lambda e, ch=ch, cb=cb: e.copy(
                        atT[:, ch * 512:(ch + 1) * 512], pC[cb][:]),
                        waits=[("T", ma)], inc="S")

                for ch in range(Tl // 512):
                    cb = ch % 2
                    mc = k.op("tensor", lambda e, ch=ch, cb=cb: e.matmul(
                        pC[cb][:], iv[:, 11, :],
                        atT[:, ch * 512:(ch + 1) * 512],
                        start=True, stop=False),
                        waits=[("V", k.cnt["V"]),
                               ("S", k.cnt.get("S", 0))], inc=None)
                    h = hp * 2 + hh
                    last = None
                    for s4 in range(4):
                        oc = ch * 4 + s4
                        for r in (0, 1, 2):
                            sc = oc + r - 1
                            if sc < 0 or sc >= NCH:
                                continue
                            last = (s4, r)
                    for s4 in range(4):
                        oc = ch * 4 + s4
                        for r in (0, 1, 2):
                            sc = oc + r - 1
                            if sc < 0 or sc >= NCH:
                                continue
                            fin = (s4, r) == last
                            mc = k.op("tensor", lambda e, s4=s4, r=r, sc=sc,
                                      h=h, cb=cb, fin=fin: e.matmul(
                                          pC[cb][:, s4 * 128:(s4 + 1) * 128],
                                          vt[:, sc, :],
                                          W["band"][:, h * 3 + r, :],
                                          start=False, stop=fin),
                                      inc="T" if fin else None)
                    gv = k.op("scalar", lambda e, ch=ch, cb=cb, P=P:
                              e.copy(
                                  ctb[P:P + 64, ch * 512:(ch + 1) * 512],
                                  pC[cb][:]),
                              waits=[("T", mc)] + ([("d_sync", ctbdma)]
                                                   if (hh == 0 and ch == 0)
                                                   else []), inc="S")
            ctbdma = k.dma("sync", aor[hp], ctb[:],
                           waits=[("V", k.cnt["V"]), ("S", k.cnt.get("S", 0))])
        k.barrier()

    with k.scope():
        xt = [k.sbuf("ny_xt", [128, 4, 1024], F32) for _ in range(2)]
        ao = [k.sbuf("ny_aot", [128, 8, 512], F32) for _ in range(2)]
        pT2 = [k.psum("ny_pT2", [128, 128], F32) for _ in range(2)]
        xr = x_in.rearrange("(ti s p) d -> ti p s d", s=4, p=128)
        xo = x_out.rearrange("(ti s p) d -> ti p s d", s=4, p=128)
        aor2 = aod.rearrange("(ob p) t -> p ob t", p=128)
        xdma = {0: 0, 1: 0}
        aofree = {0: 0, 1: 0}
        tffree = {0: ("V", 0), 1: ("V", 0)}
        for ti in range(nt):
            b = ti % 2
            ld = k.dma("sync", xt[b][:], xr[ti], waits=[("d_sync", xdma[b])])
            la = k.dma("gpsimd", ao[b][:],
                       aor2[:, :, ti * 512:(ti + 1) * 512],
                       waits=[("T", aofree[b])])
            av = 0
            for ob in range(8):
                for s in range(4):
                    tb = (ob * 4 + s) % 2
                    tv = k.op("tensor", lambda e, ob=ob, s=s, tb=tb, b=b:
                              e.transpose(
                                  pT2[tb][:], ao[b][:, ob, s * 128:(s + 1) * 128],
                                  idf[:]),
                              waits=[("d_gpsimd", la), tffree[tb]], inc="T")
                    av = k.op("vector", lambda e, ob=ob, s=s, tb=tb, b=b:
                              e.tensor_tensor(
                                  xt[b][:, s, ob * 128:(ob + 1) * 128],
                                  xt[b][:, s, ob * 128:(ob + 1) * 128],
                                  pT2[tb][:], ALU.add),
                              waits=[("T", tv), ("d_sync", ld)], inc="V")
                    tffree[tb] = ("V", av)
            xdma[b] = k.dma("sync", xo[ti], xt[b][:], waits=[("V", av)])
            aofree[b] = k.cnt["T"]
    k.barrier()


def base_inputs():
    return dict(consts=host_consts(), ident=np.eye(128, dtype=np.float32),
                onesm=np.ones((128, 128), np.float32))


def build_launch_nys():
    k = K()
    x_in = k.dram("x_in", [T, D], F32, kind="ExternalInput")
    x_out = k.dram("x_out", [T, D], F32, kind="ExternalOutput")
    C = setup_consts(k)
    W = declare_nys_weights(k, C)
    emit_nystrom(k, C, x_in, x_out, W)
    return k.finalize()


def run_launch_nys(x_full, ins, n_cores=8):
    wp = prep_weights_nys(ins)
    nc = build_launch_nys()
    shared = base_inputs()
    shared.update(wp)
    maps = []
    for c in range(n_cores):
        m = dict(shared)
        m["x_in"] = np.ascontiguousarray(x_full[c])
        maps.append(m)
    res = run_bass_kernel_spmd(nc, maps, core_ids=list(range(n_cores)))
    return np.stack([np.asarray(r["x_out"]) for r in res.results])




def prep_weights_ltm(ins):
    f = np.float32
    ct = np.ascontiguousarray
    out = {}
    out["t_WqkvT"] = ct(ins["L1_attn_W_qkv"].T.astype(f))
    out["t_WoT"] = ct(ins["L1_attn_W_o"].T.astype(f))
    wl = ins["L1_attn_W_LTM"].astype(f).reshape(8, 2, 64, 64)
    out["t_wl"] = ct(wl.transpose(1, 2, 0, 3).reshape(128, 512))
    out["t_wlT"] = ct(wl.transpose(1, 3, 0, 2).reshape(128, 512))
    def bcast(name):
        a = ins[name].astype(f).reshape(8, 2).T
        return ct(np.broadcast_to(a[:, None, :, None],
                                  (2, 64, 8, 64)).reshape(128, 512))
    out["t_vgs"] = bcast("L1_attn_V_gs")
    out["t_vt0"] = bcast("L1_attn_V_T0")
    out["t_btau"] = bcast("L1_attn_beta_tau")
    out["t_bgm"] = bcast("L1_attn_beta_gm")
    out["t_cch"] = bcast("L1_attn_C_ch")
    out["t_gam"] = bcast("L1_attn_gamma")
    out["t_alpha"] = ct(np.broadcast_to(
        ins["L1_attn_alpha_ppd"].astype(f).reshape(1, 1), (128, 1)))
    out["t_ic"] = ct(np.broadcast_to(
        ins["L1_attn_IC_threshold"].astype(f).reshape(1, 1), (128, 512)))
    blk2 = np.zeros((128, 2), f)
    blk2[0:64, 0] = 1.0
    blk2[64:128, 1] = 1.0
    out["t_blk2"] = blk2
    out["t_blk2T"] = ct(blk2.T)
    return out


def declare_ltm_weights(k, C):
    W = {}
    W["WqkvT"] = k.dram("t_WqkvT", [1024, 3072], F32, kind="ExternalInput")
    W["WoT"] = k.dram("t_WoT", [1024, 1024], F32, kind="ExternalInput")
    names = [("t_wl", "wl", [128, 512]), ("t_vgs", "vgs", [128, 512]),
             ("t_vt0", "vt0", [128, 512]), ("t_btau", "btau", [128, 512]),
             ("t_bgm", "bgm", [128, 512]), ("t_cch", "cch", [128, 512]),
             ("t_gam", "gam", [128, 512]), ("t_alpha", "alpha", [128, 1]),
             ("t_ic", "ic", [128, 512])]
    W["_pnames"] = names
    for dn, key, sh in names:
        W[key + "_d"] = k.dram(dn, sh, F32, kind="ExternalInput")
    wlT_d = k.dram("t_wlT", [128, 512], F32, kind="ExternalInput")
    W["wlT"] = k.sbuf("t_wlT_sb", [128, 512], BF16)
    k.dma("gpsimd", W["wlT"][:], wlT_d)
    b2_d = k.dram("t_blk2", [128, 2], F32, kind="ExternalInput")
    W["blk2"] = k.sbuf("t_blk2_sb", [128, 2], BF16)
    k.dma("gpsimd", W["blk2"][:], b2_d)
    b2t_d = k.dram("t_blk2T", [2, 128], F32, kind="ExternalInput")
    W["blk2T"] = k.sbuf("t_blk2T_sb", [2, 128], BF16)
    k.dma("gpsimd", W["blk2T"][:], b2t_d)
    k.barrier()
    return W


def emit_ltm(k, C, x_in, x_out, W, nt=NT, dbg=False):
    u = k.uid()
    Tl = nt * 512
    kind = "ExternalOutput" if dbg else "Internal"
    xnT_d = k.dram(f"lt_xnT{u}", [1024, T], BF16, kind=kind)
    qkvd = k.dram(f"lt_qkv{u}", [3072, T], BF16, kind=kind)
    knd = k.dram(f"lt_kn{u}", [1024, T], BF16, kind=kind)
    yd = k.dram(f"lt_y{u}", [1024, T], BF16, kind=kind)

    emit_ln(k, C, x_in, xnT_d, nt=nt)

    with k.scope():
        w_sb = k.sbuf("lt_w", [128, 8, 3072], BF16)
        rhs = [k.sbuf("lt_rhs", [128, 8, 512], BF16) for _ in range(2)]
        obuf = [k.sbuf("lt_o", [128, 24, 512], BF16) for _ in range(2)]
        ps = [k.psum("lt_ps", [128, 512], F32) for _ in range(4)]
        wv = k.dma("gpsimd", w_sb[:],
                   W["WqkvT"].rearrange("(kb p) o -> p kb o", p=128))
        srcr = xnT_d.rearrange("(kb p) t -> p kb t", p=128)
        dstr = qkvd.rearrange("(ob p) t -> p ob t", p=128)
        bankfree = {i: ("S", 0) for i in range(4)}
        rhsfree = {0: 0, 1: 0}
        odma = {0: 0, 1: 0}
        for ti in range(nt):
            b = ti % 2
            ld = k.dma("sync", rhs[b][:], srcr[:, :, ti * 512:(ti + 1) * 512],
                       waits=[("T", rhsfree[b])])
            vv = 0
            for ob in range(24):
                pb = ob % 4
                for kb in range(8):
                    wl = []
                    if kb == 0:
                        wl = [("d_sync", ld), ("d_gpsimd", wv), bankfree[pb]]
                    mmv = k.op("tensor", lambda e, kb=kb, ob=ob, b=b, pb=pb:
                               e.matmul(ps[pb][:],
                                        w_sb[:, kb, ob * 128:(ob + 1) * 128],
                                        rhs[b][:, kb, :], start=(kb == 0),
                                        stop=(kb == 7)),
                               waits=wl, inc="T" if kb == 7 else None)
                vv = k.op("scalar", lambda e, ob=ob, b=b, pb=pb: e.copy(
                    obuf[b][:, ob, :], ps[pb][:]),
                    waits=[("T", mmv)] + ([("d_sync", odma[b])]
                                          if ob == 0 else []), inc="S")
                bankfree[pb] = ("S", vv)
            odma[b] = k.dma("sync", dstr[:, :, ti * 512:(ti + 1) * 512],
                            obuf[b][:], waits=[("S", vv)])
            rhsfree[b] = k.cnt["T"]
    k.barrier()

    outer = k.scope()
    outer.__enter__()
    rt = k.sbuf("lt_rt", [128, 512], F32)
    ct_ = k.sbuf("lt_ct", [128, 512], F32)
    with k.scope():
        W = dict(W)
        dv0 = k.cnt.get("d_sync", 0)
        for dn, key, sh in W["_pnames"]:
            t = k.sbuf(dn + "_sb", sh, F32)
            dv0 = k.dma("sync", t[:], W[key + "_d"])
            W[key] = t
        vn = k.sbuf("lt_vn", [128, 512], F32)
        t1 = k.sbuf("lt_t1", [128, 512], F32)
        t2 = k.sbuf("lt_t2", [128, 512], F32)
        sp = k.sbuf("lt_sp", [128, 512], F32)
        g = k.sbuf("lt_g", [128, 512], F32)
        h0 = k.op("vector", lambda e, W=W: e.tensor_tensor(
            vn[:], W["wl"][:], W["vgs"][:], ALU.add),
            waits=[("d_sync", dv0)], inc="V")
        h1 = k.op("vector", lambda e: e.tensor_tensor(
            vn[:], vn[:], W["vt0"][:], ALU.subtract), waits=[("V", h0)],
            inc="V")
        s0 = k.op("scalar", lambda e: e.activation(
            t1[:], vn[:], AF.Abs), waits=[("V", h1)], inc="S")
        s1 = k.op("scalar", lambda e: e.activation(
            t1[:], t1[:], AF.Exp, scale=-1.0), waits=[("S", s0)], inc="S")
        s2 = k.op("scalar", lambda e: e.activation(
            t1[:], t1[:], AF.Ln, bias=1.0), waits=[("S", s1)], inc="S")
        h2 = k.op("vector", lambda e: e.tensor_scalar(
            t2[:], vn[:], 0.0, None, ALU.max), waits=[("V", h1)], inc="V")
        h3 = k.op("vector", lambda e: e.tensor_tensor(
            sp[:], t1[:], t2[:], ALU.add), waits=[("S", s2), ("V", h2)],
            inc="V")
        h4 = k.op("vector", lambda e: e.tensor_tensor(
            g[:], W["btau"][:], sp[:], ALU.mult), waits=[("V", h3)], inc="V")
        h5 = k.op("vector", lambda e: e.reciprocal(t1[:], W["cch"][:]),
                  inc="V")
        h6 = k.op("vector", lambda e: e.tensor_tensor(
            t2[:], g[:], t1[:], ALU.mult), waits=[("V", h4), ("V", h5)],
            inc="V")
        s3 = k.op("scalar", lambda e: e.activation(
            rt[:], t2[:], AF.Exp, scale=-1.0), waits=[("V", h6)], inc="S")
        s4 = k.op("scalar", lambda e: e.activation(
            t1[:], vn[:], AF.Sigmoid), waits=[("V", h6)], inc="S")
        h7 = k.op("vector", lambda e: e.tensor_tensor(
            t1[:], t1[:], sp[:], ALU.mult), waits=[("S", s4)], inc="V")
        h8 = k.op("vector", lambda e: e.tensor_tensor(
            t1[:], t1[:], W["bgm"][:], ALU.mult), waits=[("V", h7)], inc="V")
        h9 = k.op("vector", lambda e: e.tensor_tensor(
            t2[:], g[:], W["ic"][:], ALU.subtract), waits=[("V", h8)],
            inc="V")
        s5 = k.op("scalar", lambda e: e.activation(
            t2[:], t2[:], AF.Tanh, scale=W["alpha"][:, 0:1]),
            waits=[("V", h9)], inc="S")
        h10 = k.op("vector", lambda e: e.tensor_tensor(
            t2[:], t2[:], W["gam"][:], ALU.mult), waits=[("S", s5)], inc="V")
        k.op("vector", lambda e: e.tensor_tensor(
            ct_[:], t2[:], t1[:], ALU.mult), waits=[("V", h10)], inc="V")
    k.barrier()

    with k.scope():
        kt2 = k.sbuf("lt_kt2", [128, Tl], BF16)
        ksq = k.sbuf("lt_ksq", [128, Tl], BF16)
        kn2 = k.sbuf("lt_kn2", [128, Tl], BF16)
        stm = k.sbuf("lt_stm", [2, 512], F32)
        stq = k.sbuf("lt_stq", [2, 512], F32)
        mt2 = k.sbuf("lt_mt2", [2, 512], F32)
        mb = k.sbuf("lt_mb", [2, 512], BF16)
        rb = k.sbuf("lt_rb", [2, 512], BF16)
        tmpc = k.sbuf("lt_tmpc", [128, 512], BF16)
        pA = k.psum("lt_pA", [2, 512], F32)
        pB = k.psum("lt_pB", [2, 512], F32)
        pC1 = k.psum("lt_pC1", [128, 512], F32)
        pC2 = k.psum("lt_pC2", [128, 512], F32)
        kr = qkvd.rearrange("(g hp p) t -> g hp p t", g=3, p=128)
        knr = knd.rearrange("(hp p) t -> hp p t", p=128)
        for hp in range(8):
            lk = k.dma("sync", kt2[:], kr[1][hp],
                       waits=[("T", k.cnt["T"]), ("V", k.cnt["V"]),
                              ("d_sync", k.cnt["d_sync"])])
            sq0 = k.op("scalar", lambda e: e.activation(
                ksq[:], kt2[:], AF.Square), waits=[("d_sync", lk),
                                                   ("T", k.cnt["T"])], inc="S")
            for ch in range(Tl // 512):
                c0 = ch * 512
                m1 = k.op("tensor", lambda e, c0=c0: e.matmul(
                    pA[:], W["blk2"][:], kt2[:, c0:c0 + 512],
                    start=True, stop=True),
                    waits=[("S", sq0), ("V", k.cnt["V"])], inc="T")
                m2 = k.op("tensor", lambda e, c0=c0: e.matmul(
                    pB[:], W["blk2"][:], ksq[:, c0:c0 + 512],
                    start=True, stop=True), inc="T")
                a1 = k.op("vector", lambda e: e.tensor_scalar(
                    stm[:], pA[:], 1.0 / 64.0, None, ALU.mult),
                    waits=[("T", m1)], inc="V")
                a2 = k.op("vector", lambda e: e.tensor_scalar(
                    stq[:], pB[:], 1.0 / 64.0, None, ALU.mult),
                    waits=[("T", m2)], inc="V")
                a3 = k.op("vector", lambda e: e.tensor_tensor(
                    mt2[:], stm[:], stm[:], ALU.mult),
                    waits=[("V", a1)], inc="V")
                a4 = k.op("vector", lambda e: e.tensor_tensor(
                    stq[:], stq[:], mt2[:], ALU.subtract),
                    waits=[("V", a2), ("V", a3)], inc="V")
                sb1 = k.op("scalar", lambda e: e.activation(
                    stq[:], stq[:], AF.Ln, bias=1e-5), waits=[("V", a4)],
                    inc="S")
                sb2 = k.op("scalar", lambda e: e.activation(
                    stq[:], stq[:], AF.Exp, scale=-0.5), waits=[("S", sb1)],
                    inc="S")
                b1 = k.op("vector", lambda e: e.tensor_copy(mb[:], stm[:]),
                          waits=[("S", sb2)], inc="V")
                b2 = k.op("vector", lambda e: e.tensor_copy(rb[:], stq[:]),
                          waits=[("V", b1)], inc="V")
                m3 = k.op("tensor", lambda e: e.matmul(
                    pC1[:], W["blk2T"][:], mb[:], start=True, stop=True),
                    waits=[("V", b2)], inc="T")
                m4 = k.op("tensor", lambda e: e.matmul(
                    pC2[:], W["blk2T"][:], rb[:], start=True, stop=True),
                    inc="T")
                c1 = k.op("vector", lambda e, c0=c0: e.tensor_tensor(
                    tmpc[:], kt2[:, c0:c0 + 512], pC1[:], ALU.subtract),
                    waits=[("T", m4)], inc="V")
                c2 = k.op("vector", lambda e, c0=c0: e.tensor_tensor(
                    kn2[:, c0:c0 + 512], tmpc[:], pC2[:], ALU.mult),
                    waits=[("V", c1)], inc="V")
            k.dma("sync", knr[hp], kn2[:], waits=[("V", k.cnt["V"])])
    k.barrier()

    CHK = 1024
    nch = Tl // CHK
    with k.scope():
        v2 = k.sbuf("lt_v2", [128, Tl], BF16)
        q2 = k.sbuf("lt_q2", [128, Tl], BF16)
        y = k.sbuf("lt_y", [128, Tl], F32)
        kg = [k.sbuf("lt_kg", [128, 8, CHK], BF16) for _ in range(2)]
        qg = [k.sbuf("lt_qg", [128, 8, CHK], BF16) for _ in range(2)]
        zz = [k.sbuf("lt_z", [128, CHK], F32) for _ in range(2)]
        uu = k.sbuf("lt_u", [128, CHK], F32)
        tb1 = k.sbuf("lt_tb1", [128, CHK], BF16)
        tb2 = k.sbuf("lt_tb2", [128, CHK], BF16)
        tq = k.sbuf("lt_tq", [128, CHK], BF16)
        carry = k.sbuf("lt_carry", [128, 64], F32)
        pW = k.psum("lt_pW", [128, 512], F32)
        qkr = qkvd.rearrange("(g hp p) t -> g hp p t", g=3, p=128)
        ydr = yd.rearrange("(hp p) t -> hp p t", p=128)
        for hp in range(8):
            lv = k.dma("sync", v2[:], qkr[2][hp],
                       waits=[("d_gpsimd", k.cnt.get("d_gpsimd", 0)),
                              ("V", k.cnt["V"]), ("G", k.cnt.get("G", 0))])
            lq = k.dma("sync", q2[:], qkr[0][hp],
                       waits=[("V", k.cnt["V"]), ("T", k.cnt["T"])])
            for h2 in range(2):
                P = h2 * 64
                for c5 in range(Tl // 512):
                    mw = k.op("tensor", lambda e, P=P, c5=c5, hp=hp: e.matmul(
                        pW[P:P + 64, :],
                        W["wlT"][P:P + 64, hp * 64:(hp + 1) * 64],
                        q2[P:P + 64, c5 * 512:(c5 + 1) * 512],
                        start=True, stop=True),
                        waits=[("d_sync", lq), ("V", k.cnt["V"]),
                               ("S", k.cnt.get("S", 0))], inc="T")
                    k.op("scalar", lambda e, P=P, c5=c5: e.copy(
                        y[P:P + 64, c5 * 512:(c5 + 1) * 512], pW[P:P + 64, :]),
                        waits=[("T", mw)] + ([("d_gpsimd",
                                              k.cnt.get("d_gpsimd", 0))]
                                             if (h2 == 0 and c5 == 0)
                                             else []), inc="S")
            kqfree = {0: k.cnt["V"], 1: k.cnt["V"]}
            kgfree = {0: k.cnt.get("G", 0), 1: k.cnt.get("G", 0)}
            zfree = {0: 0, 1: 0}
            for ch in range(nch):
                t0 = ch * CHK
                for gr in range(8):
                    gb = gr % 2
                    lk2 = lq2 = 0
                    for a in range(2):
                        srck = bass.AP(
                            knd.tensor,
                            (hp * 128 + a * 64 + gr * 8) * T + t0,
                            [[0, 64], [T, 8], [1, CHK]])
                        srcq = bass.AP(
                            qkvd.tensor,
                            (hp * 128 + a * 64 + gr * 8) * T + t0,
                            [[0, 64], [T, 8], [1, CHK]])
                        lk2 = k.dma("scalar", kg[gb][a * 64:(a + 1) * 64],
                                    srck, waits=[("G", kgfree[gb])]
                                    if a == 0 else ())
                        lq2 = k.dma("sync", qg[gb][a * 64:(a + 1) * 64],
                                    srcq, waits=[("V", kqfree[gb])]
                                    if a == 0 else ())
                    vv = 0
                    for ee in range(8):
                        e_ = gr * 8 + ee
                        col = hp * 64 + e_
                        r_ap = bass.AP(rt.ap().tensor, col, [[512, 128],
                                                             [0, CHK]])
                        wz = [("d_scalar", lk2)] if ee == 0 else []
                        if ch == 0 and gr == 0 and ee == 0:
                            wz.append(("d_sync", lv))
                        wz.append(("V", zfree[ee % 2]))
                        zv = k.op("gpsimd", lambda e, ee=ee, gb=gb, t0=t0,
                                  zb=ee % 2: e.tensor_tensor(
                                      zz[zb][:], v2[:, t0:t0 + CHK],
                                      kg[gb][:, ee, :], ALU.mult),
                                  waits=wz, inc="G")
                        if ch == 0:
                            sc = k.op("vector", lambda e, r_ap=r_ap,
                                      zb=ee % 2: e.tensor_tensor_scan(
                                          uu[:], r_ap, zz[zb][:], 0.0,
                                          ALU.mult, ALU.add),
                                      waits=[("G", zv)], inc="V")
                        else:
                            sc = k.op("vector", lambda e, r_ap=r_ap, e_=e_,
                                      zb=ee % 2: e.tensor_tensor_scan(
                                          uu[:], r_ap, zz[zb][:],
                                          carry[:, e_:e_ + 1],
                                          ALU.mult, ALU.add),
                                      waits=[("G", zv)], inc="V")
                        zfree[ee % 2] = sc
                        cy = k.op("vector", lambda e, e_=e_: e.tensor_copy(
                            carry[:, e_:e_ + 1], uu[:, CHK - 1:CHK]),
                            waits=[("V", sc)], inc="V")
                        ts1 = k.op("vector", lambda e, col=col: e.tensor_scalar(
                            tb1[:], uu[:], ct_[:, col:col + 1], None,
                            ALU.mult), waits=[("V", cy)], inc="V")
                        dst = tq if (ee % 4) == 0 else tb2
                        tv = k.op("vector", lambda e, ee=ee, gb=gb, dst=dst:
                                  e.tensor_tensor(
                                      dst[:], tb1[:], qg[gb][:, ee, :],
                                      ALU.mult),
                                  waits=[("V", ts1), ("d_sync", lq2)],
                                  inc="V")
                        if ee % 4 != 0:
                            tv = k.op("vector", lambda e: e.tensor_tensor(
                                tq[:], tq[:], tb2[:], ALU.add),
                                waits=[("V", tv)], inc="V")
                        if ee % 4 == 3:
                            vv = k.op("vector", lambda e, t0=t0: e.tensor_tensor(
                                y[:, t0:t0 + CHK], y[:, t0:t0 + CHK],
                                tq[:], ALU.add), waits=[("V", tv)], inc="V")
                    kqfree[gb] = vv
                    kgfree[gb] = k.cnt.get("G", 0)
            k.dma("gpsimd", ydr[hp], y[:], waits=[("V", k.cnt["V"])])
    k.barrier()
    outer.__exit__(None, None, None)

    with k.scope():
        idf = C["idf"]
        xt = [k.sbuf("lt_xt", [128, 4, 1024], F32) for _ in range(2)]
        tmp = k.sbuf("lt_tmp", [128, 512], F32)
        TF = [k.psum("lt_TF", [128, 128], F32) for _ in range(2)]
        xr = x_in.rearrange("(ti s p) d -> ti p s d", s=4, p=128)
        xo = x_out.rearrange("(ti s p) d -> ti p s d", s=4, p=128)
        st4 = dict(xdma={0: 0, 1: 0}, tffree={0: ("V", 0), 1: ("V", 0)},
                   tmpfree=("T", 0), ld=0)

        def evac4(ti, ob, psum, mmv):
            b = ti % 2
            if ob == 0:
                st4["ld"] = k.dma("sync", xt[b][:], xr[ti],
                                  waits=[("d_sync", st4["xdma"][b])])
            v1 = k.op("vector", lambda e, ob=ob: e.tensor_copy(
                tmp[:], psum[:]),
                waits=[("T", mmv), st4["tmpfree"]], inc="V")
            av = 0
            for s in range(4):
                pb = s % 2
                tv = k.op("tensor", lambda e, s=s, pb=pb: e.transpose(
                    TF[pb][:], tmp[:, s * 128:(s + 1) * 128], idf[:]),
                    waits=[("V", v1), st4["tffree"][pb]], inc="T")
                av = k.op("vector", lambda e, s=s, ob=ob, b=b, pb=pb:
                          e.tensor_tensor(
                              xt[b][:, s, ob * 128:(ob + 1) * 128],
                              xt[b][:, s, ob * 128:(ob + 1) * 128],
                              TF[pb][:], ALU.add),
                          waits=[("T", tv)] + ([("d_sync", st4["ld"])]
                                               if (s == 0 and ob == 0) else []),
                          inc="V")
                st4["tffree"][pb] = ("V", av)
            st4["tmpfree"] = ("T", k.cnt["T"])
            if ob == 7:
                st4["xdma"][b] = k.dma("sync", xo[ti], xt[b][:],
                                       waits=[("V", av)])
            return ("V", v1)

        emit_mm_stage(k, C, yd, W["WoT"], 1024, (0, 1024), evac4, nt=nt)


def build_launch_ltm():
    k = K()
    x_in = k.dram("x_in", [T, D], F32, kind="ExternalInput")
    x_out = k.dram("x_out", [T, D], F32, kind="ExternalOutput")
    C = setup_consts(k)
    W = declare_ltm_weights(k, C)
    emit_ltm(k, C, x_in, x_out, W)
    return k.finalize()


def run_launch_ltm(x_full, ins, n_cores=8):
    wp = prep_weights_ltm(ins)
    nc = build_launch_ltm()
    shared = base_inputs()
    shared.update(wp)
    maps = []
    for c in range(n_cores):
        m = dict(shared)
        m["x_in"] = np.ascontiguousarray(x_full[c])
        maps.append(m)
    res = run_bass_kernel_spmd(nc, maps, core_ids=list(range(n_cores)))
    return np.stack([np.asarray(r["x_out"]) for r in res.results])




def build_fused(dbg=False):
    k = K()
    x0 = k.dram("x_in", [T, D], F32, kind="ExternalInput")
    x6 = k.dram("x_out", [T, D], F32, kind="ExternalOutput")
    kind = "ExternalOutput" if dbg else "Internal"
    xs = [k.dram(f"x_s{i}", [T, D], F32, kind=kind) for i in range(1, 6)]
    C = setup_consts(k)
    Wf = declare_l0ffn_weights(k, C)
    Wn = declare_nys_weights(k, C)
    Wm = declare_moe_weights(k, C)
    Wt = declare_ltm_weights(k, C)
    emit_l0ffn(k, C, x0, xs[0], Wf)
    emit_nystrom(k, C, xs[0], xs[1], Wn)
    emit_l0ffn(k, C, xs[1], xs[2], Wf)
    emit_moe(k, C, xs[2], xs[3], Wm)
    emit_ltm(k, C, xs[3], xs[4], Wt, dbg=dbg)
    emit_moe(k, C, xs[4], x6, Wm)
    return k.finalize()


_FUSED_CACHE = {}


def kernel_fused(**inputs):
    ins = inputs
    x = np.ascontiguousarray(np.asarray(ins["L0_attn_hidden_states"],
                                        dtype=np.float32))
    if "nc" not in _FUSED_CACHE:
        _FUSED_CACHE["nc"] = build_fused()
    nc = _FUSED_CACHE["nc"]
    shared = base_inputs()
    shared.update(prep_weights_l0ffn(ins))
    shared.update(prep_weights_nys(ins))
    shared.update(prep_weights_moe(ins))
    shared.update(prep_weights_ltm(ins))
    maps = []
    for c in range(8):
        m = dict(shared)
        m["x_in"] = np.ascontiguousarray(x[c])
        maps.append(m)
    res = run_bass_kernel_spmd(nc, maps, core_ids=list(range(8)))
    out = np.stack([np.asarray(r["x_out"], dtype=np.float32)
                    for r in res.results])
    return np.ascontiguousarray(out)
